# version 03
# baseline (speedup 1.0000x reference)
import math
import numpy as np
import contextlib
import concourse.bass as bass
import concourse.mybir as mybir

F32 = mybir.dt.float32
BF16 = mybir.dt.bfloat16
FP8 = mybir.dt.float8e4
AF = mybir.ActivationFunctionType
ALU = mybir.AluOpType
AX = mybir.AxisListType

D = 2048
T = 2048
NH = 16
HD = 128
HID = 8192
NFC = D // 128
NTT = T // 128
NOWN = 8
TOWN = NOWN * 128
NHC = HID // 128
HCG = 4
NGRP = NHC // HCG
ISC = float(np.float32(1.0 / math.sqrt(HD)))

ENGINES = ("gpsimd", "tensor", "vector", "scalar", "sync")


class Prog:
    def __init__(self):
        self.items = {e: [] for e in ENGINES}
        self.tags = {}
        self.counts = {e: 0 for e in ENGINES}
        self.nbar = 0

    def op(self, engine, fn, waits=(), tag=None, dma=False):
        inc = 0
        if tag is not None or dma:
            inc = 16 if dma else 1
            self.counts[engine] += inc
            if tag is not None:
                assert tag not in self.tags, tag
                self.tags[tag] = (engine, self.counts[engine], dma)
        ws = tuple(w for w in waits if w is not None)
        for w in ws:
            assert w in self.tags, f"wait on unknown tag {w}"
        self.items[engine].append((fn, ws, inc))

    def alias(self, new_tag, old_tag):
        assert new_tag not in self.tags
        self.tags[new_tag] = self.tags[old_tag]

    def barrier(self, name, tags):
        for e in ENGINES:
            self.op(e, "W", waits=list(tags))

    def flush(self, nc, sems):
        if not hasattr(self, "hw"):
            self.hw = {e: {o: 0 for o in ENGINES} for e in ENGINES}
        items = {e: self.items[e] for e in ENGINES}
        self.items = {e: [] for e in ENGINES}
        if all(not v for v in items.values()):
            return
        with nc.Block() as block:
            def mk(engine):
                def body(eng):
                    hw = self.hw[engine]
                    for fn, waits, inc in items[engine]:
                        for tg in waits:
                            pe, cnt, isdma = self.tags[tg]
                            if cnt > hw[pe]:
                                eng.wait_ge(sems[pe], cnt)
                                hw[pe] = cnt
                        if fn is None:
                            eng.sem_inc(sems[engine], 1)
                        elif fn == "W":
                            pass
                        else:
                            ins = fn(eng)
                            if inc:
                                ins.then_inc(sems[engine], inc)
                return body
            for e in ENGINES:
                if items[e]:
                    getattr(block, e)(mk(e))


def ap3(t, off, dims):
    return bass.AP(t, off, dims)


def ap3r(t, off, dims):
    return bass.AP(t, off, dims, dep_tracking_offset=0)


def build_core_program(p, dbg=()):
    nc = bass.Bass("TRN2", target_bir_lowering=False)

    xp_h = nc.dram_tensor("xp", [T, D], F32, kind="ExternalInput").ap()
    wqkv_h = nc.dram_tensor("wqkv", [D, 3 * D], F32, kind="ExternalInput").ap()
    wo_h = nc.dram_tensor("wo", [D, D], F32, kind="ExternalInput").ap()
    wg_h = nc.dram_tensor("wg", [2, D, HID], F32, kind="ExternalInput").ap()
    wl_h = nc.dram_tensor("wl", [HID, D], F32, kind="ExternalInput").ap()
    cos_h = nc.dram_tensor("cosf", [128, T], F32, kind="ExternalInput").ap()
    sin_h = nc.dram_tensor("sinf", [128, T], F32, kind="ExternalInput").ap()
    perm_h = nc.dram_tensor("perms", [128, 128], F32, kind="ExternalInput").ap()
    id_h = nc.dram_tensor("ident", [128, 128], F32, kind="ExternalInput").ap()
    mka_h = nc.dram_tensor("maska", [128, 128], F32, kind="ExternalInput").ap()
    mkb_h = nc.dram_tensor("maskb", [128, 128], F32, kind="ExternalInput").ap()

    out_h = nc.dram_tensor("out", [TOWN, D], F32, kind="ExternalOutput").ap()
    x1_h = nc.dram_tensor("x1scr", [TOWN, D], F32, kind="Internal").ap()

    dbg_aps = {}
    def dbg_t(name, shape, dt):
        if name in dbg:
            dbg_aps[name] = nc.dram_tensor("dbg_" + name, shape, dt, kind="ExternalOutput").ap()
        return dbg_aps.get(name)

    dbg_xnT = dbg_t("xnT", [128, NFC, T], BF16)
    dbg_kt = dbg_t("kt", [128, T], BF16)
    dbg_qt = dbg_t("qt", [128, TOWN], BF16)
    dbg_av = dbg_t("av", [128, NH, TOWN], BF16)
    dbg_xn2 = dbg_t("xn2", [128, NFC, TOWN], FP8)
    dbg_den = dbg_t("den", [128, NH, NOWN], F32)
    dbg_pt = dbg_t("pt", [128, NTT, 128], BF16)

    P = Prog()
    es = contextlib.ExitStack()
    with es:
        sems = {e: es.enter_context(nc.semaphore(f"sem_{e}")) for e in ENGINES}
        sb = lambda n, s, d: es.enter_context(nc.sbuf_tensor(n, s, d))

        ident = sb("identb", [128, 128], BF16)
        perms = sb("permsb", [128, 128], BF16)
        maska = sb("maskab", [128, 128], BF16)
        maskb = sb("maskbb", [128, 128], BF16)
        av_all = sb("av_all", [128, NH, TOWN], BF16)
        xn2T = sb("xn2Ts", [128, NFC, TOWN], FP8)

        P.op("gpsimd", lambda g: g.dma_start(ident[:, :], id_h), dma=True, tag="ld_id")
        P.op("gpsimd", lambda g: g.dma_start(perms[:, :], perm_h), dma=True, tag="ld_pm")
        P.op("gpsimd", lambda g: g.dma_start(maska[:, :], mka_h), dma=True, tag="ld_ma")
        P.op("gpsimd", lambda g: g.dma_start(maskb[:, :], mkb_h), dma=True, tag="ld_mb")

        with contextlib.ExitStack() as s1:
            sb1 = lambda n, s, d: s1.enter_context(nc.sbuf_tensor(n, s, d))
            xnT = sb1("xnT", [128, NFC, T], BF16)
            cosf = sb1("cosfb", [128, T], F32)
            sinf = sb1("sinfb", [128, T], F32)
            wq = [sb1(f"wq{i}", [128, NFC, 128], BF16) for i in range(2)]
            wk = [sb1(f"wk{i}", [128, NFC, 128], BF16) for i in range(2)]
            wv = [sb1(f"wv{i}", [128, NFC, 128], BF16) for i in range(2)]

            def wcol(col0):
                return wqkv_h[:, col0:col0 + 128].rearrange("(n p) c -> p n c", p=128)


            P.op("gpsimd", lambda g: g.dma_start(cosf[:, :], cos_h), dma=True, tag="ld_cos")
            P.op("gpsimd", lambda g: g.dma_start(sinf[:, :], sin_h), dma=True, tag="ld_sin")
            for h in range(2):
                P.op("gpsimd", lambda g, h=h: g.dma_start(
                    wq[h][:, :, :], wcol(h * 128)), dma=True, tag=f"b_wq{h}")
                P.op("gpsimd", lambda g, h=h: g.dma_start(
                    wk[h][:, :, :], wcol(D + h * 128)), dma=True, tag=f"b_wk{h}")
                P.op("gpsimd", lambda g, h=h: g.dma_start(
                    wv[h][:, :, :], wcol(2 * D + h * 128)), dma=True, tag=f"b_wv{h}")

            with contextlib.ExitStack() as sa:
                sba = lambda n, s, d: sa.enter_context(nc.sbuf_tensor(n, s, d))
                psa = lambda n, s, d: sa.enter_context(nc.psum_tensor(n, s, d))
                xst = sba("xst", [128, 2, D], F32)
                xnst = sba("xnst", [128, 2, D], BF16)
                sqscr = sba("sqscr", [128, D], F32)
                stats = sba("stats", [128, 8], F32)
                pA = [psa(f"pA{i}", [128, 128], BF16) for i in range(2)]
                trbank_last_ev = [None, None]

                for tt in range(NTT):
                    sl = tt % 2
                    P.op("sync", lambda g, tt=tt, sl=sl: g.dma_start(
                        xst[:, sl, :], xp_h[tt * 128:(tt + 1) * 128, :]),
                        waits=[f"a_norm{tt-2}"] if tt >= 2 else [],
                        dma=True, tag=f"a_x{tt}")
                    P.op("scalar", lambda s, sl=sl: s.activation(
                        sqscr[:, :], xst[:, sl, :], AF.Square, accum_out=stats[:, 1:2]),
                        waits=[f"a_x{tt}", f"a_ex2_{tt-1}" if tt >= 1 else None],
                        tag=f"a_sq{tt}")
                    P.op("vector", lambda v, sl=sl: v.reduce_sum(
                        stats[:, 0:1], xst[:, sl, :], axis=AX.X), waits=[f"a_x{tt}"],
                        tag=f"a_r{tt}")
                    P.op("vector", lambda v: v.tensor_scalar_mul(stats[:, 2:3], stats[:, 0:1], 1.0 / D),
                         waits=[f"a_r{tt}"], tag=f"a_c2_{tt}")
                    P.op("vector", lambda v: v.tensor_scalar_mul(stats[:, 3:4], stats[:, 1:2], 1.0 / D),
                         waits=[f"a_sq{tt}"], tag=f"a_ex2_{tt}")
                    P.op("vector", lambda v: v.tensor_mul(stats[:, 4:5], stats[:, 2:3], stats[:, 2:3]),
                         waits=[f"a_c2_{tt}"], tag=f"a_c4_{tt}")
                    P.op("vector", lambda v: v.tensor_sub(stats[:, 5:6], stats[:, 3:4], stats[:, 4:5]),
                         waits=[f"a_ex2_{tt}", f"a_c4_{tt}"], tag=f"a_c5_{tt}")
                    P.op("vector", lambda v: v.tensor_scalar_add(stats[:, 6:7], stats[:, 5:6], 1e-5),
                         waits=[f"a_c5_{tt}"], tag=f"a_c6_{tt}")
                    P.op("vector", lambda v: v.reciprocal(stats[:, 7:8], stats[:, 6:7]),
                         waits=[f"a_c6_{tt}"], tag=f"a_rec{tt}")
                    P.op("scalar", lambda s: s.activation(stats[:, 6:7], stats[:, 7:8], AF.Sqrt),
                         waits=[f"a_rec{tt}"], tag=f"a_rstd{tt}")
                    P.op("vector", lambda v, sl=sl: v.tensor_scalar(
                        xnst[:, sl, :], xst[:, sl, :], stats[:, 2:3], stats[:, 6:7],
                        ALU.subtract, ALU.mult),
                        waits=[f"a_rstd{tt}", f"a_c2_{tt}",
                               f"a_tr{tt-2}_{NFC-1}" if tt >= 2 else None],
                        tag=f"a_norm{tt}")
                    for fc in range(NFC):
                        bk = fc % 2
                        P.op("tensor", lambda t, sl=sl, fc=fc, bk=bk: t.transpose(
                            pA[bk][:, :], xnst[:, sl, fc * 128:(fc + 1) * 128], ident[:, :]),
                            waits=[f"a_norm{tt}", "ld_id", trbank_last_ev[bk]],
                            tag=f"a_tr{tt}_{fc}")
                        P.op("vector", lambda v, tt=tt, fc=fc, bk=bk: v.tensor_copy(
                            xnT[:, fc, tt * 128:(tt + 1) * 128], pA[bk][:, :]),
                            waits=[f"a_tr{tt}_{fc}"], tag=f"a_tev{tt}_{fc}")
                        trbank_last_ev[bk] = f"a_tev{tt}_{fc}"
                if dbg_xnT is not None:
                    P.op("sync", lambda g: g.dma_start(dbg_xnT, xnT[:, :, :]),
                         waits=[f"a_tev{NTT-1}_{NFC-1}"], dma=True, tag="dbg_xnT")
                P.barrier("barA", [f"a_tev{NTT-1}_{NFC-1}", f"a_rstd{NTT-1}",
                                   f"a_tr{NTT-1}_{NFC-1}", f"a_x{NTT-1}"]
                          + (["dbg_xnT"] if dbg_xnT is not None else []))
                P.flush(nc, sems)

            with contextlib.ExitStack() as s2:
                sbb = lambda n, s, d: s2.enter_context(nc.sbuf_tensor(n, s, d))
                psb = lambda n, s, d: s2.enter_context(nc.psum_tensor(n, s, d))
                KT = [sbb(f"KT{i}", [128, T], BF16) for i in range(2)]
                QT = [sbb(f"QT{i}", [128, TOWN], BF16) for i in range(2)]
                Vo = [sbb(f"Vo{i}", [128, NTT, 130], BF16) for i in range(2)]
                vT = sbb("vT", [128, T], BF16)
                Pt = [sbb(f"Pt{i}", [128, NTT, 128], BF16) for i in range(2)]
                avtok = sbb("avtok", [128, NOWN, 128], BF16)
                raw = sbb("raw", [128, 512], BF16)
                m1 = sbb("m1", [128, 512], F32)
                m2 = sbb("m2", [128, 512], F32)
                dinv = sbb("dinv", [128, 1], F32)
                dens = sbb("dens", [128, NH, NOWN], F32) if dbg_den is not None else None

                bA = [psb(f"bA{i}", [128, 512], F32) for i in range(2)]
                bB = psb("bB", [128, 512], F32)
                pST = [psb(f"pST{i}", [128, 2, 128], F32) for i in range(2)]
                pAV = psb("pAV", [128, 132], F32)
                pTR = [psb(f"pTR{i}", [128, 128], BF16) for i in range(2)]

                for i in range(2):
                    P.op("vector", lambda v, i=i: v.memset(
                        ap3(Vo[i], 128, [[NTT * 130, 128], [130, NTT], [1, 1]]), 1.0),
                        tag=f"b_ones{i}")

                bA_last_rd = [None, None]
                bB_last_rd = [None]
                raw_last_rd = [None]
                vT_last_rd = [None] * 4
                pST_last_rd = [None, None]
                pAV_last_rd = [None]
                pTR_last_rd = [None, None]
                Pt_last_rd = [None, None]
                KTQT_last_rd = [None, None]
                Vo_last_rd = [None, None]
                avtok_last_rd = [None] * NOWN
                wbuf_last_rd = [None, None]
                ck = [0]
                spc = [0]

                def proj_rope(h, hb, kind, wbuf, wtag, nchunks, rhs_of, cs_of, dst_of, dst_guard):
                    for c in range(nchunks):
                        bi = ck[0] % 2
                        ba = bA[bi]
                        for fc in range(NFC):
                            P.op("tensor",
                                 lambda t, ba=ba, rhs=rhs_of(c, fc), fc=fc, wbuf=wbuf: t.matmul(
                                     ba[:, :], wbuf[:, fc, :], rhs,
                                     start=(fc == 0), stop=(fc == NFC - 1)),
                                 waits=[wtag, bA_last_rd[bi]] if fc == 0 else [],
                                 tag=f"b_{kind}fl{h}_{c}" if fc == NFC - 1 else None)
                        P.op("vector", lambda v, ba=ba: v.tensor_copy(raw[:, :], ba[:, :]),
                             waits=[f"b_{kind}fl{h}_{c}", raw_last_rd[0]],
                             tag=f"b_{kind}rw{h}_{c}")
                        P.op("tensor", lambda t: t.matmul(
                            bB[:, :], perms[:, :], raw[:, :], start=True, stop=True),
                            waits=[f"b_{kind}rw{h}_{c}", "ld_pm", bB_last_rd[0]],
                            tag=f"b_{kind}sw{h}_{c}")
                        raw_last_rd[0] = f"b_{kind}sw{h}_{c}"
                        cc, ss = cs_of(c)
                        ba3 = ap3(ba, 0, [[512, 128], [128, 4], [1, 128]])
                        bB3 = ap3(bB, 0, [[512, 128], [128, 4], [1, 128]])
                        m13 = ap3(m1, 0, [[512, 128], [128, 4], [1, 128]])
                        m23 = ap3(m2, 0, [[512, 128], [128, 4], [1, 128]])
                        P.op("vector", lambda v, m13=m13, ba3=ba3, cc=cc: v.tensor_tensor(
                            m13, ba3, cc, ALU.mult),
                            waits=["ld_cos"], tag=f"b_{kind}m1_{h}_{c}")
                        bA_last_rd[bi] = f"b_{kind}m1_{h}_{c}"
                        P.op("vector", lambda v, m23=m23, bB3=bB3, ss=ss: v.tensor_tensor(
                            m23, bB3, ss, ALU.mult),
                            waits=[f"b_{kind}sw{h}_{c}", "ld_sin"], tag=f"b_{kind}m2_{h}_{c}")
                        bB_last_rd[0] = f"b_{kind}m2_{h}_{c}"
                        P.op("vector", lambda v, dst=dst_of(c): v.tensor_add(
                            dst, m1[:, :], m2[:, :]),
                            waits=[dst_guard, f"b_{kind}m1_{h}_{c}", f"b_{kind}m2_{h}_{c}"],
                            tag=f"b_{kind}dn{h}_{c}")
                        ck[0] += 1

                for h in range(NH):
                    hb = h % 2
                    if h >= 2:
                        P.op("gpsimd", lambda g, h=h, hb=hb: g.dma_start(
                            wq[hb][:, :, :], wcol(h * 128)),
                            waits=[wbuf_last_rd[hb]], dma=True, tag=f"b_wq{h}")
                        P.op("gpsimd", lambda g, h=h, hb=hb: g.dma_start(
                            wk[hb][:, :, :], wcol(D + h * 128)), dma=True, tag=f"b_wk{h}")
                        P.op("gpsimd", lambda g, h=h, hb=hb: g.dma_start(
                            wv[hb][:, :, :], wcol(2 * D + h * 128)), dma=True, tag=f"b_wv{h}")

                    proj_rope(
                        h, hb, "k", wk[hb], f"b_wk{h}", 4,
                        rhs_of=lambda c, fc: xnT[:, fc, c * 512:(c + 1) * 512],
                        cs_of=lambda c: (ap3(cosf, c * 512, [[T, 128], [128, 4], [1, 128]]),
                                         ap3(sinf, c * 512, [[T, 128], [128, 4], [1, 128]])),
                        dst_of=lambda c: KT[hb][:, c * 512:(c + 1) * 512],
                        dst_guard=KTQT_last_rd[hb])

                    proj_rope(
                        h, hb, "q", wq[hb], f"b_wq{h}", 2,
                        rhs_of=lambda c, fc: xnT[:, fc, c * 512:(c + 1) * 512],
                        cs_of=lambda c: (ap3(cosf, c * 512, [[T, 128], [128, 4], [1, 128]]),
                                         ap3(sinf, c * 512, [[T, 128], [128, 4], [1, 128]])),
                        dst_of=lambda c: QT[hb][:, c * 512:(c + 1) * 512],
                        dst_guard=KTQT_last_rd[hb])

                    for c in range(4):
                        bi = ck[0] % 2
                        ba = bA[bi]
                        for fc in range(NFC):
                            P.op("tensor", lambda t, ba=ba, fc=fc, c=c, hb=hb: t.matmul(
                                ba[:, :], wv[hb][:, fc, :], xnT[:, fc, c * 512:(c + 1) * 512],
                                start=(fc == 0), stop=(fc == NFC - 1)),
                                waits=[f"b_wv{h}", bA_last_rd[bi]] if fc == 0 else [],
                                tag=f"b_vfl{h}_{c}" if fc == NFC - 1 else None)
                        P.op("vector", lambda v, ba=ba, c=c: v.tensor_copy(
                            vT[:, c * 512:(c + 1) * 512], ba[:, :]),
                            waits=[f"b_vfl{h}_{c}", vT_last_rd[c]], tag=f"b_vev{h}_{c}")
                        bA_last_rd[bi] = f"b_vev{h}_{c}"
                        ck[0] += 1
                    for i in range(NTT):
                        bi = i % 2
                        P.op("tensor", lambda t, i=i, bi=bi: t.transpose(
                            pTR[bi][:, :], vT[:, i * 128:(i + 1) * 128], ident[:, :]),
                            waits=[f"b_vev{h}_{i // 4}", pTR_last_rd[bi]], tag=f"b_vtr{h}_{i}")
                        vT_last_rd[i // 4] = f"b_vtr{h}_{i}"
                        P.op("vector", lambda v, i=i, hb=hb, bi=bi: v.tensor_copy(
                            Vo[hb][:, i, 0:128], pTR[bi][:, :]),
                            waits=[f"b_vtr{h}_{i}", f"b_ones{hb}",
                                   Vo_last_rd[hb] if i == 0 else None],
                            tag=f"b_vte{h}_{i}")
                        pTR_last_rd[bi] = f"b_vte{h}_{i}"
                    wbuf_last_rd[hb] = f"b_vfl{h}_{3}"

                    for j in range(NOWN):
                        tlist = [t for pr in range(j + 1) for t in (pr, NOWN + pr)]
                        ntiles = 2 * j + 2
                        for pr in range(j + 1):
                            i0, i1 = pr, NOWN + pr
                            sbk = spc[0] % 2
                            stb = pST[sbk]
                            fw = [pST_last_rd[sbk]]
                            if j == 0 and pr == 0:
                                fw += [f"b_kdn{h}_{3}", f"b_qdn{h}_{1}", f"b_vte{h}_{NTT-1}"]
                            P.op("tensor", lambda t, stb=stb, i0=i0, j=j, hb=hb: t.matmul(
                                stb[:, 0, :], KT[hb][:, i0 * 128:(i0 + 1) * 128],
                                QT[hb][:, j * 128:(j + 1) * 128], start=True, stop=True),
                                waits=fw)
                            P.op("tensor", lambda t, stb=stb, i1=i1, j=j, hb=hb: t.matmul(
                                stb[:, 1, :], KT[hb][:, i1 * 128:(i1 + 1) * 128],
                                QT[hb][:, j * 128:(j + 1) * 128], start=True, stop=True),
                                tag=f"b_sf{h}_{j}_{pr}")
                            KTQT_last_rd[hb] = f"b_sf{h}_{j}_{pr}"
                            stb2 = ap3(stb, 0, [[256, 128], [1, 256]])
                            P.op("scalar", lambda s, j=j, i0=i0, stb2=stb2: s.activation(
                                ap3(Pt[j % 2], i0 * 128,
                                    [[NTT * 128, 128], [NOWN * 128, 2], [1, 128]]),
                                stb2, AF.Exp, scale=ISC),
                                waits=[f"b_sf{h}_{j}_{pr}",
                                       Pt_last_rd[j % 2] if pr == 0 else None],
                                tag=f"b_ex{h}_{j}_{pr}")
                            pST_last_rd[sbk] = f"b_ex{h}_{j}_{pr}"
                            spc[0] += 1
                        P.op("vector", lambda v, j=j: v.tensor_mul(
                            Pt[j % 2][:, j, :], Pt[j % 2][:, j, :], maska[:, :]),
                            waits=[f"b_ex{h}_{j}_{j}", "ld_ma"])
                        P.op("vector", lambda v, j=j: v.tensor_mul(
                            Pt[j % 2][:, NOWN + j, :], Pt[j % 2][:, NOWN + j, :], maskb[:, :]),
                            waits=["ld_mb"], tag=f"b_mk{h}_{j}")
                        for idx, i in enumerate(tlist):
                            pr = idx // 2
                            avw = []
                            if idx == 0:
                                avw.append(pAV_last_rd[0])
                            if i in (j, NOWN + j):
                                avw.append(f"b_mk{h}_{j}")
                            else:
                                avw.append(f"b_ex{h}_{j}_{pr}")
                            P.op("tensor", lambda t, i=i, j=j, hb=hb, st=(idx == 0),
                                 sp=(idx == ntiles - 1): t.matmul(
                                pAV[:, 0:129], Pt[j % 2][:, i, :], Vo[hb][:, i, 0:129],
                                start=st, stop=sp),
                                waits=avw, tag=f"b_av{h}_{j}" if idx == ntiles - 1 else None)
                        Pt_last_rd[j % 2] = f"b_av{h}_{j}"
                        Vo_last_rd[hb] = f"b_av{h}_{j}"
                        if dens is not None:
                            P.op("vector", lambda v, h=h, j=j: v.tensor_copy(
                                dens[:, h, j:j + 1], pAV[:, 128:129]),
                                waits=[f"b_av{h}_{j}"])
                        P.op("vector", lambda v: v.reciprocal(dinv[:, :], pAV[:, 128:129]),
                             waits=[f"b_av{h}_{j}"], tag=f"b_rc{h}_{j}")
                        P.op("vector", lambda v, j=j: v.tensor_scalar(
                            avtok[:, j, :], pAV[:, 0:128], dinv[:, :], None, ALU.mult),
                            waits=[avtok_last_rd[j], f"b_rc{h}_{j}"], tag=f"b_ae{h}_{j}")
                        pAV_last_rd[0] = f"b_ae{h}_{j}"
                    for j in range(NOWN):
                        bi = j % 2
                        P.op("tensor", lambda t, j=j, bi=bi: t.transpose(
                            pTR[bi][:, :], avtok[:, j, :], ident[:, :]),
                            waits=[f"b_ae{h}_{j}", pTR_last_rd[bi]], tag=f"b_at{h}_{j}")
                        avtok_last_rd[j] = f"b_at{h}_{j}"
                        P.op("vector", lambda v, j=j, h=h, bi=bi: v.tensor_copy(
                            av_all[:, h, j * 128:(j + 1) * 128], pTR[bi][:, :]),
                            waits=[f"b_at{h}_{j}"], tag=f"b_ate{h}_{j}")
                        pTR_last_rd[bi] = f"b_ate{h}_{j}"

                if dbg_den is not None:
                    P.op("sync", lambda g: g.dma_start(dbg_den, dens[:, :, :]),
                         waits=[f"b_ae{NH-1}_{NOWN-1}"], dma=True, tag="dbg_den")
                if dbg_pt is not None:
                    P.op("sync", lambda g: g.dma_start(dbg_pt, Pt[1][:, :, :]),
                         waits=[f"b_ae{NH-1}_{NOWN-1}"], dma=True, tag="dbg_pt")
                if dbg_kt is not None:
                    P.op("sync", lambda g: g.dma_start(dbg_kt, KT[(NH - 1) % 2][:, :]),
                         waits=[f"b_kdn{NH-1}_{3}"], dma=True, tag="dbg_kt")
                if dbg_qt is not None:
                    P.op("sync", lambda g: g.dma_start(dbg_qt, QT[(NH - 1) % 2][:, :]),
                         waits=[f"b_qdn{NH-1}_{1}"], dma=True, tag="dbg_qt")
                if dbg_av is not None:
                    P.op("sync", lambda g: g.dma_start(dbg_av, av_all[:, :, :]),
                         waits=[f"b_ate{NH-1}_{NOWN-1}"], dma=True, tag="dbg_av")
                P.barrier("barB", [f"b_ate{NH-1}_{NOWN-1}", f"b_ex{NH-1}_{NOWN-1}_{NOWN-1}",
                                   f"b_at{NH-1}_{NOWN-1}", f"b_wv{NH-1}"]
                          + [t for t in ("dbg_kt", "dbg_qt", "dbg_av", "dbg_den", "dbg_pt") if t in P.tags])
                P.flush(nc, sems)

        with contextlib.ExitStack() as s3:
            sbc = lambda n, s, d: s3.enter_context(nc.sbuf_tensor(n, s, d))
            psc = lambda n, s, d: s3.enter_context(nc.psum_tensor(n, s, d))
            wob = sbc("wob", [128, NFC, 4, 512], BF16)
            xstage = sbc("xstage", [128, 2, D], F32)
            x1t = sbc("x1t", [128, 2, D], F32)
            xn2st = sbc("xn2st", [128, 2, D], BF16)
            sqscr2 = sbc("sqscr2", [128, D], F32)
            stats2 = sbc("stats2", [128, 8], F32)
            pC = [psc(f"pC{i}", [128, 512], F32) for i in range(2)]
            pTc = [psc(f"pTc{i}", [128, 128], BF16) for i in range(2)]

            for dc in range(4):
                P.op("gpsimd", lambda g, dc=dc: g.dma_start(
                    wob[:, :, dc, :],
                    wo_h[:, dc * 512:(dc + 1) * 512].rearrange("(n p) c -> p n c", p=128)),
                    dma=True, tag=f"c_wo{dc}")

            pC_last_rd = [None, None]
            pTc_last_rd = [None, None]
            xn2st_last_rd = [None, None]
            def emit_tr(j):
                sl = j % 2
                for fc in range(NFC):
                    bi = fc % 2
                    P.op("tensor", lambda t, sl=sl, fc=fc, bi=bi, j=j: t.transpose(
                        pTc[bi][:, :], xn2st[:, sl, fc * 128:(fc + 1) * 128], ident[:, :]),
                        waits=[f"d_norm{j}", pTc_last_rd[bi]], tag=f"d_tr{j}_{fc}")
                    P.op("vector", lambda v, j=j, fc=fc, bi=bi: v.tensor_copy(
                        xn2T[:, fc, j * 128:(j + 1) * 128], pTc[bi][:, :]),
                        waits=[f"d_tr{j}_{fc}"], tag=f"d_te{j}_{fc}")
                    pTc_last_rd[bi] = f"d_te{j}_{fc}"
                xn2st_last_rd[sl] = f"d_tr{j}_{NFC-1}"

            for j in range(NOWN):
                sl = j % 2
                P.op("sync", lambda sy, sl=sl, j=j: sy.dma_start(
                    xstage[:, sl, :], xp_h[j * 128:(j + 1) * 128, :]),
                    waits=[f"c_ev{j-2}_{3}"] if j >= 2 else [], dma=True, tag=f"c_x{j}")
                for dc in range(4):
                    bi = dc % 2
                    fw = [f"c_wo{dc}", pC_last_rd[bi], f"c_x{j}"]
                    for h in range(NH):
                        P.op("tensor", lambda t, h=h, j=j, dc=dc, bi=bi: t.matmul(
                            pC[bi][:, :], av_all[:, h, j * 128:(j + 1) * 128],
                            wob[:, h, dc, :], start=(h == 0), stop=(h == NH - 1)),
                            waits=fw if h == 0 else [],
                            tag=f"c_fl{j}_{dc}" if h == NH - 1 else None)
                    P.op("vector", lambda v, j=j, dc=dc, bi=bi, sl=sl: v.tensor_add(
                        x1t[:, sl, dc * 512:(dc + 1) * 512], pC[bi][:, :],
                        xstage[:, sl, dc * 512:(dc + 1) * 512]),
                        waits=[f"c_fl{j}_{dc}",
                               f"d_norm{j-2}" if (j >= 2 and dc == 0) else None,
                               f"c_st{j-2}" if (j >= 2 and dc == 0) else None],
                        tag=f"c_ev{j}_{dc}")
                    pC_last_rd[bi] = f"c_ev{j}_{dc}"
                P.op("sync", lambda sy, j=j, sl=sl: sy.dma_start(
                    x1_h[j * 128:(j + 1) * 128, :], x1t[:, sl, :]),
                    waits=[f"c_ev{j}_{3}"], dma=True, tag=f"c_st{j}")
                P.op("scalar", lambda s, sl=sl: s.activation(
                    sqscr2[:, :], x1t[:, sl, :], AF.Square, accum_out=stats2[:, 1:2]),
                    waits=[f"c_ev{j}_{3}", f"d_ex2_{j-1}" if j >= 1 else None],
                    tag=f"d_sq{j}")
                P.op("vector", lambda v, sl=sl: v.reduce_sum(
                    stats2[:, 0:1], x1t[:, sl, :], axis=AX.X),
                    waits=[f"c_ev{j}_{3}"], tag=f"d_r{j}")
                P.op("vector", lambda v: v.tensor_scalar_mul(stats2[:, 2:3], stats2[:, 0:1], 1.0 / D),
                     waits=[f"d_r{j}"], tag=f"d_c2_{j}")
                P.op("vector", lambda v: v.tensor_scalar_mul(stats2[:, 3:4], stats2[:, 1:2], 1.0 / D),
                     waits=[f"d_sq{j}"], tag=f"d_ex2_{j}")
                P.op("vector", lambda v: v.tensor_mul(stats2[:, 4:5], stats2[:, 2:3], stats2[:, 2:3]),
                     waits=[f"d_c2_{j}"], tag=f"d_c4_{j}")
                P.op("vector", lambda v: v.tensor_sub(stats2[:, 5:6], stats2[:, 3:4], stats2[:, 4:5]),
                     waits=[f"d_ex2_{j}", f"d_c4_{j}"], tag=f"d_c5_{j}")
                P.op("vector", lambda v: v.tensor_scalar_add(stats2[:, 6:7], stats2[:, 5:6], 1e-5),
                     waits=[f"d_c5_{j}"], tag=f"d_c6_{j}")
                P.op("vector", lambda v: v.reciprocal(stats2[:, 7:8], stats2[:, 6:7]),
                     waits=[f"d_c6_{j}"], tag=f"d_rec{j}")
                P.op("scalar", lambda s: s.activation(stats2[:, 6:7], stats2[:, 7:8], AF.Sqrt),
                     waits=[f"d_rec{j}"], tag=f"d_rstd{j}")
                P.op("vector", lambda v, sl=sl: v.tensor_scalar(
                    xn2st[:, sl, :], x1t[:, sl, :], stats2[:, 2:3], stats2[:, 6:7],
                    ALU.subtract, ALU.mult),
                    waits=[f"d_rstd{j}", f"d_c2_{j}", xn2st_last_rd[sl]], tag=f"d_norm{j}")
                if j >= 1:
                    emit_tr(j - 1)
            emit_tr(NOWN - 1)
            if dbg_xn2 is not None:
                P.op("sync", lambda g: g.dma_start(dbg_xn2, xn2T[:, :, :]),
                     waits=[f"d_te{NOWN-1}_{NFC-1}"], dma=True, tag="dbg_xn2")
            P.barrier("barC", [f"d_te{NOWN-1}_{NFC-1}", f"d_rstd{NOWN-1}",
                               f"d_tr{NOWN-1}_{NFC-1}", f"c_st{NOWN-1}", f"c_wo{3}"]
                      + (["dbg_xn2"] if dbg_xn2 is not None else []))
            P.flush(nc, sems)

        with contextlib.ExitStack() as s4:
            sbe = lambda n, s, d: s4.enter_context(nc.sbuf_tensor(n, s, d))
            pse = lambda n, s, d: s4.enter_context(nc.psum_tensor(n, s, d))
            act = sbe("acts", [128, NHC, TOWN], FP8)
            wgb = [[sbe(f"wg{g}_{q}", [128, NFC, HCG * 128], FP8) for q in range(2)] for g in range(2)]
            wlb = [sbe(f"wl{q}", [128, NHC, 256], FP8) for q in range(2)]
            gate = sbe("gate", [128, 2, TOWN], BF16)
            outs = sbe("outs", [128, 2, 256], F32)
            x1p = sbe("x1p", [128, 2, 256], F32)
            pg = [pse(f"pg{i}", [128, 512], F32) for i in range(4)]
            pl = [pse(f"pl{i}", [128, 256], F32) for i in range(4)]

            def gfill(grp, hc_i, gg, n):
                return f"e_gf{grp}_{hc_i}_{gg}_{n}"

            pg_last_rd = [None] * 4
            gate_last_rd = [None, None]
            for grp in range(NGRP):
                for gg in range(2):
                    P.op("gpsimd", lambda g, grp=grp, gg=gg: g.dma_start(
                        wgb[gg][grp % 2][:, :, :],
                        wg_h[gg, :, grp * (HCG * 128):(grp + 1) * (HCG * 128)]
                        .rearrange("(n p) c -> p n c", p=128)),
                        waits=[gfill(grp - 2, HCG - 1, gg, 1) if grp >= 2 else None],
                        dma=True, tag=f"e_wg{grp}_{gg}")
                for hc_i in range(HCG):
                    hc = grp * HCG + hc_i
                    for gg in range(2):
                        for n in range(2):
                            bi = gg * 2 + n
                            fw = [f"e_wg{grp}_{gg}", pg_last_rd[bi]]
                            if grp == 0 and hc_i == 0 and gg == 0 and n == 0:
                                fw.append(f"d_te{NOWN-1}_{NFC-1}")
                            for fc in range(NFC):
                                P.op("tensor",
                                     lambda t, bi=bi, fc=fc, n=n, grp=grp, hc_i=hc_i, gg=gg: t.matmul(
                                         pg[bi][:, :],
                                         wgb[gg][grp % 2][:, fc, hc_i * 128:(hc_i + 1) * 128],
                                         xn2T[:, fc, n * 512:(n + 1) * 512],
                                         start=(fc == 0), stop=(fc == NFC - 1)),
                                     waits=fw if fc == 0 else [],
                                     tag=gfill(grp, hc_i, gg, n) if fc == NFC - 1 else None)
                    for n in range(2):
                        P.op("scalar", lambda s, n=n, hc=hc: s.activation(
                            gate[:, hc % 2, n * 512:(n + 1) * 512], pg[n][:, :],
                            AF.Gelu_apprx_tanh),
                            waits=[gfill(grp, hc_i, 0, n),
                                   gate_last_rd[hc % 2] if n == 0 else None],
                            tag=f"e_gl{grp}_{hc_i}_{n}")
                        pg_last_rd[n] = f"e_gl{grp}_{hc_i}_{n}"
                        P.op("vector", lambda v, n=n, hc=hc: v.tensor_tensor(
                            act[:, hc, n * 512:(n + 1) * 512],
                            gate[:, hc % 2, n * 512:(n + 1) * 512],
                            pg[2 + n][:, :], ALU.mult),
                            waits=[gfill(grp, hc_i, 1, n), f"e_gl{grp}_{hc_i}_{n}"],
                            tag=f"e_ac{grp}_{hc_i}_{n}")
                        pg_last_rd[2 + n] = f"e_ac{grp}_{hc_i}_{n}"
                    gate_last_rd[hc % 2] = f"e_ac{grp}_{hc_i}_{1}"

            pl_last_rd = [None] * 4
            outs_last_rd = [None, None]
            lf = 0
            for dc in range(8):
                P.op("gpsimd", lambda g, dc=dc: g.dma_start(
                    wlb[dc % 2][:, :, :],
                    wl_h[:, dc * 256:(dc + 1) * 256].rearrange("(n p) c -> p n c", p=128)),
                    waits=[f"e_lf{dc-2}_{NOWN-1}" if dc >= 2 else None],
                    dma=True, tag=f"e_wl{dc}")
                for tt in range(NOWN):
                    bi = lf % 4
                    fw = [f"e_wl{dc}", pl_last_rd[bi]]
                    if dc == 0 and tt == 0:
                        fw.append(f"e_ac{NGRP-1}_{HCG-1}_{1}")
                    for hc in range(NHC):
                        P.op("tensor", lambda t, hc=hc, tt=tt, dc=dc, bi=bi: t.matmul(
                            pl[bi][:, :], act[:, hc, tt * 128:(tt + 1) * 128],
                            wlb[dc % 2][:, hc, :], start=(hc == 0), stop=(hc == NHC - 1)),
                            waits=fw if hc == 0 else [],
                            tag=f"e_lf{dc}_{tt}" if hc == NHC - 1 else None)
                    P.op("sync", lambda sy, tt=tt, dc=dc, sl=lf % 2: sy.dma_start(
                        x1p[:, sl, :], x1_h[tt * 128:(tt + 1) * 128, dc * 256:(dc + 1) * 256]),
                        waits=[f"c_st{tt}", f"e_ev{lf-2}" if lf >= 2 else None],
                        dma=True, tag=f"e_x1{lf}")
                    P.op("vector", lambda v, bi=bi, sl=lf % 2: v.tensor_add(
                        outs[:, sl, :], pl[bi][:, :], x1p[:, sl, :]),
                        waits=[f"e_lf{dc}_{tt}", f"e_x1{lf}", outs_last_rd[lf % 2]],
                        tag=f"e_ev{lf}")
                    pl_last_rd[bi] = f"e_ev{lf}"
                    P.op("sync", lambda sy, tt=tt, dc=dc, sl=lf % 2: sy.dma_start(
                        out_h[tt * 128:(tt + 1) * 128, dc * 256:(dc + 1) * 256],
                        outs[:, sl, :]),
                        waits=[f"e_ev{lf}"], dma=True, tag=f"e_st{lf}")
                    outs_last_rd[lf % 2] = f"e_st{lf}"
                    lf += 1
            P.op("gpsimd", "W", waits=[f"e_st{63}"])
            P.op("vector", "W", waits=[f"e_st{63}"])
            P.op("tensor", "W", waits=[f"e_st{63}"])
            P.op("scalar", "W", waits=[f"e_st{63}"])
            P.flush(nc, sems)

    return nc, sorted(dbg_aps.keys())


def host_constants():
    inv_freq = (1.0 / (10000.0 ** (np.arange(0, HD, 2, dtype=np.float32) / HD))).astype(np.float32)
    t = np.arange(T, dtype=np.float32)
    freqs = np.outer(t, inv_freq).astype(np.float32)
    cos64 = np.cos(freqs).astype(np.float32)
    sin64 = np.sin(freqs).astype(np.float32)
    cosf = np.ascontiguousarray(np.concatenate([cos64, cos64], axis=1).T)
    sinf = np.ascontiguousarray(np.concatenate([sin64, sin64], axis=1).T)
    perms = np.zeros((128, 128), dtype=np.float32)
    for m in range(64):
        perms[m + 64, m] = -1.0
    for m in range(64, 128):
        perms[m - 64, m] = 1.0
    ident = np.eye(128, dtype=np.float32)
    tri = (np.arange(128)[:, None] <= np.arange(128)[None, :]).astype(np.float32)
    return cosf, sinf, perms, ident, tri


def core_inputs(p, xb, wqkv, wo, wg, wl):
    cosf, sinf, perms, ident, tri = host_constants()
    perm = np.concatenate([np.arange(16)[p::2], np.arange(16)[1 - p::2]])
    xp = np.ascontiguousarray(xb.reshape(16, 128, D)[perm].reshape(T, D))
    cosp = np.ascontiguousarray(cosf.reshape(128, 16, 128)[:, perm].reshape(128, T))
    sinp = np.ascontiguousarray(sinf.reshape(128, 16, 128)[:, perm].reshape(128, T))
    maska = tri
    maskb = np.full((128, 128), float(p), dtype=np.float32)
    return {
        "xp": xp, "wqkv": wqkv, "wo": wo, "wg": wg, "wl": wl,
        "cosf": cosp, "sinf": sinp, "perms": perms, "ident": ident,
        "maska": maska, "maskb": maskb,
    }


_NC_CACHE = {}


def _get_nc():
    if "nc" not in _NC_CACHE:
        _NC_CACHE["nc"] = build_core_program(0, dbg=())[0]
    return _NC_CACHE["nc"]


def kernel(L0_attn_x, L0_attn_w_qkv, L0_attn_w_o, L0_ffn_w_gating, L0_ffn_w_linear):
    from concourse.bass_utils import run_bass_kernel_spmd
    x = np.ascontiguousarray(np.asarray(L0_attn_x, dtype=np.float32))
    wqkv = np.ascontiguousarray(np.asarray(L0_attn_w_qkv, dtype=np.float32))
    wo = np.ascontiguousarray(np.asarray(L0_attn_w_o, dtype=np.float32))
    wg = np.ascontiguousarray(np.asarray(L0_ffn_w_gating, dtype=np.float32))
    wl = np.ascontiguousarray(np.asarray(L0_ffn_w_linear, dtype=np.float32))

    nc = _get_nc()
    in_maps = []
    for c in range(8):
        b, p = c // 2, c % 2
        in_maps.append(core_inputs(p, np.ascontiguousarray(x[b]), wqkv, wo, wg, wl))
    res = run_bass_kernel_spmd(nc, in_maps, core_ids=list(range(8)))

    out = np.empty((4, 2048, 2048), dtype=np.float32)
    for c in range(8):
        b, p = c // 2, c % 2
        o = np.asarray(res.results[c]["out"]).reshape(NOWN, 128, D)
        out[b].reshape(16, 128, D)[p::2] = o
    return out


# version 04
# speedup vs baseline: 1.1307x; 1.1307x over previous
import math
import numpy as np
import contextlib
import concourse.bass as bass
import concourse.mybir as mybir

F32 = mybir.dt.float32
BF16 = mybir.dt.bfloat16
FP8 = mybir.dt.float8e4
AF = mybir.ActivationFunctionType
ALU = mybir.AluOpType
AX = mybir.AxisListType

D = 2048
T = 2048
NH = 16
HD = 128
HID = 8192
NFC = D // 128
NTT = T // 128
NOWN = 8
TOWN = NOWN * 128
NHC = HID // 128
HCG = 4
NGRP = NHC // HCG
ISC = float(np.float32(1.0 / math.sqrt(HD)))

ENGINES = ("gpsimd", "tensor", "vector", "scalar", "sync")


class Prog:
    def __init__(self):
        self.items = {e: [] for e in ENGINES}
        self.tags = {}
        self.counts = {e: 0 for e in ENGINES}
        self.nbar = 0

    def op(self, engine, fn, waits=(), tag=None, dma=False):
        inc = 0
        if tag is not None or dma:
            inc = 16 if dma else 1
            self.counts[engine] += inc
            if tag is not None:
                assert tag not in self.tags, tag
                self.tags[tag] = (engine, self.counts[engine], dma)
        ws = tuple(w for w in waits if w is not None)
        for w in ws:
            assert w in self.tags, f"wait on unknown tag {w}"
        self.items[engine].append((fn, ws, inc))

    def alias(self, new_tag, old_tag):
        assert new_tag not in self.tags
        self.tags[new_tag] = self.tags[old_tag]

    def barrier(self, name, tags):
        for e in ENGINES:
            self.op(e, "W", waits=list(tags))

    def flush(self, nc, sems):
        if not hasattr(self, "hw"):
            self.hw = {e: {o: 0 for o in ENGINES} for e in ENGINES}
        items = {e: self.items[e] for e in ENGINES}
        self.items = {e: [] for e in ENGINES}
        if all(not v for v in items.values()):
            return
        with nc.Block() as block:
            def mk(engine):
                def body(eng):
                    hw = self.hw[engine]
                    for fn, waits, inc in items[engine]:
                        for tg in waits:
                            pe, cnt, isdma = self.tags[tg]
                            if cnt > hw[pe]:
                                eng.wait_ge(sems[pe], cnt)
                                hw[pe] = cnt
                        if fn is None:
                            eng.sem_inc(sems[engine], 1)
                        elif fn == "W":
                            pass
                        else:
                            ins = fn(eng)
                            if inc:
                                ins.then_inc(sems[engine], inc)
                return body
            for e in ENGINES:
                if items[e]:
                    getattr(block, e)(mk(e))


def ap3(t, off, dims):
    return bass.AP(t, off, dims)


def ap3r(t, off, dims):
    return bass.AP(t, off, dims, dep_tracking_offset=0)


def build_core_program(p, dbg=()):
    nc = bass.Bass("TRN2", target_bir_lowering=False)

    xp_h = nc.dram_tensor("xp", [T, D], F32, kind="ExternalInput").ap()
    wqi_h = nc.dram_tensor("wqi", [NH, 128, NFC, 128], F32, kind="ExternalInput").ap()
    wki_h = nc.dram_tensor("wki", [NH, 128, NFC, 128], F32, kind="ExternalInput").ap()
    wvi_h = nc.dram_tensor("wvi", [NH, 128, NFC, 128], F32, kind="ExternalInput").ap()
    woi_h = nc.dram_tensor("woi", [4, 128, NFC, 512], F32, kind="ExternalInput").ap()
    wgi_h = nc.dram_tensor("wgi", [NGRP, 2, 128, NFC, HCG * 128], F32, kind="ExternalInput").ap()
    wli_h = nc.dram_tensor("wli", [8, 128, NHC, 256], F32, kind="ExternalInput").ap()
    cos_h = nc.dram_tensor("cosf", [128, T], F32, kind="ExternalInput").ap()
    sin_h = nc.dram_tensor("sinf", [128, T], F32, kind="ExternalInput").ap()
    perm_h = nc.dram_tensor("perms", [128, 128], F32, kind="ExternalInput").ap()
    id_h = nc.dram_tensor("ident", [128, 128], F32, kind="ExternalInput").ap()
    mka_h = nc.dram_tensor("maska", [128, 256], F32, kind="ExternalInput").ap()
    mkb_h = nc.dram_tensor("maskb", [128, 256], F32, kind="ExternalInput").ap()
    mkc_h = nc.dram_tensor("maskc", [128, 256], F32, kind="ExternalInput").ap()
    mkd_h = nc.dram_tensor("maskd", [128, 256], F32, kind="ExternalInput").ap()

    out_h = nc.dram_tensor("out", [TOWN, D], F32, kind="ExternalOutput").ap()
    x1_h = nc.dram_tensor("x1scr", [TOWN, D], F32, kind="Internal").ap()

    dbg_aps = {}
    def dbg_t(name, shape, dt):
        if name in dbg:
            dbg_aps[name] = nc.dram_tensor("dbg_" + name, shape, dt, kind="ExternalOutput").ap()
        return dbg_aps.get(name)

    dbg_xnT = dbg_t("xnT", [128, NFC, T], BF16)
    dbg_kt = dbg_t("kt", [128, T], BF16)
    dbg_qt = dbg_t("qt", [128, TOWN], BF16)
    dbg_av = dbg_t("av", [128, NH, TOWN], BF16)
    dbg_xn2 = dbg_t("xn2", [128, NFC, TOWN], FP8)
    dbg_den = dbg_t("den", [128, NH, NOWN], F32)
    dbg_pt = dbg_t("pt", [128, NTT, 256], BF16)

    P = Prog()
    es = contextlib.ExitStack()
    with es:
        sems = {e: es.enter_context(nc.semaphore(f"sem_{e}")) for e in ENGINES}
        sb = lambda n, s, d: es.enter_context(nc.sbuf_tensor(n, s, d))

        ident = sb("identb", [128, 128], BF16)
        perms = sb("permsb", [128, 128], BF16)
        maska = sb("maskab", [128, 256], BF16)
        maskb = sb("maskbb", [128, 256], BF16)
        maskc = sb("maskcb", [128, 256], BF16)
        maskd = sb("maskdb", [128, 256], BF16)
        av_all = sb("av_all", [128, NH, TOWN], BF16)
        xn2T = sb("xn2Ts", [128, NFC, TOWN], FP8)

        P.op("gpsimd", lambda g: g.dma_start(ident[:, :], id_h), dma=True, tag="ld_id")
        P.op("gpsimd", lambda g: g.dma_start(perms[:, :], perm_h), dma=True, tag="ld_pm")
        P.op("gpsimd", lambda g: g.dma_start(maska[:, :], mka_h), dma=True, tag="ld_ma")
        P.op("gpsimd", lambda g: g.dma_start(maskb[:, :], mkb_h), dma=True, tag="ld_mb")
        P.op("gpsimd", lambda g: g.dma_start(maskc[:, :], mkc_h), dma=True, tag="ld_mc")
        P.op("gpsimd", lambda g: g.dma_start(maskd[:, :], mkd_h), dma=True, tag="ld_md")

        with contextlib.ExitStack() as s1:
            sb1 = lambda n, s, d: s1.enter_context(nc.sbuf_tensor(n, s, d))
            xnT = sb1("xnT", [128, NFC, T], BF16)
            cosf = sb1("cosfb", [128, T], F32)
            sinf = sb1("sinfb", [128, T], F32)
            wq = [sb1(f"wq{i}", [128, NFC, 128], BF16) for i in range(2)]
            wk = [sb1(f"wk{i}", [128, NFC, 128], BF16) for i in range(2)]
            wv = [sb1(f"wv{i}", [128, NFC, 128], BF16) for i in range(2)]


            P.op("gpsimd", lambda g: g.dma_start(cosf[:, :], cos_h), dma=True, tag="ld_cos")
            P.op("gpsimd", lambda g: g.dma_start(sinf[:, :], sin_h), dma=True, tag="ld_sin")
            for h in range(2):
                P.op("gpsimd", lambda g, h=h: g.dma_start(
                    wq[h][:, :, :], wqi_h[h]), dma=True, tag=f"b_wq{h}")
                P.op("gpsimd", lambda g, h=h: g.dma_start(
                    wk[h][:, :, :], wki_h[h]), dma=True, tag=f"b_wk{h}")
                P.op("gpsimd", lambda g, h=h: g.dma_start(
                    wv[h][:, :, :], wvi_h[h]), dma=True, tag=f"b_wv{h}")

            with contextlib.ExitStack() as sa:
                sba = lambda n, s, d: sa.enter_context(nc.sbuf_tensor(n, s, d))
                psa = lambda n, s, d: sa.enter_context(nc.psum_tensor(n, s, d))
                xst = sba("xst", [128, 2, D], F32)
                xnst = sba("xnst", [128, 2, D], BF16)
                sqscr = sba("sqscr", [128, D], F32)
                stats = sba("stats", [128, 8], F32)
                pA = [psa(f"pA{i}", [128, 128], BF16) for i in range(2)]
                trbank_last_ev = [None, None]

                for tt in range(NTT):
                    sl = tt % 2
                    P.op("sync", lambda g, tt=tt, sl=sl: g.dma_start(
                        xst[:, sl, :], xp_h[tt * 128:(tt + 1) * 128, :]),
                        waits=[f"a_norm{tt-2}"] if tt >= 2 else [],
                        dma=True, tag=f"a_x{tt}")
                    P.op("scalar", lambda s, sl=sl: s.activation(
                        sqscr[:, :], xst[:, sl, :], AF.Square, accum_out=stats[:, 1:2]),
                        waits=[f"a_x{tt}", f"a_ex2_{tt-1}" if tt >= 1 else None],
                        tag=f"a_sq{tt}")
                    P.op("vector", lambda v, sl=sl: v.reduce_sum(
                        stats[:, 0:1], xst[:, sl, :], axis=AX.X), waits=[f"a_x{tt}"],
                        tag=f"a_r{tt}")
                    P.op("vector", lambda v: v.tensor_scalar_mul(stats[:, 2:3], stats[:, 0:1], 1.0 / D),
                         waits=[f"a_r{tt}"], tag=f"a_c2_{tt}")
                    P.op("vector", lambda v: v.tensor_scalar_mul(stats[:, 3:4], stats[:, 1:2], 1.0 / D),
                         waits=[f"a_sq{tt}"], tag=f"a_ex2_{tt}")
                    P.op("vector", lambda v: v.tensor_mul(stats[:, 4:5], stats[:, 2:3], stats[:, 2:3]),
                         waits=[f"a_c2_{tt}"], tag=f"a_c4_{tt}")
                    P.op("vector", lambda v: v.tensor_sub(stats[:, 5:6], stats[:, 3:4], stats[:, 4:5]),
                         waits=[f"a_ex2_{tt}", f"a_c4_{tt}"], tag=f"a_c5_{tt}")
                    P.op("vector", lambda v: v.tensor_scalar_add(stats[:, 6:7], stats[:, 5:6], 1e-5),
                         waits=[f"a_c5_{tt}"], tag=f"a_c6_{tt}")
                    P.op("vector", lambda v: v.reciprocal(stats[:, 7:8], stats[:, 6:7]),
                         waits=[f"a_c6_{tt}"], tag=f"a_rec{tt}")
                    P.op("scalar", lambda s: s.activation(stats[:, 6:7], stats[:, 7:8], AF.Sqrt),
                         waits=[f"a_rec{tt}"], tag=f"a_rstd{tt}")
                    P.op("vector", lambda v, sl=sl: v.tensor_scalar(
                        xnst[:, sl, :], xst[:, sl, :], stats[:, 2:3], stats[:, 6:7],
                        ALU.subtract, ALU.mult),
                        waits=[f"a_rstd{tt}", f"a_c2_{tt}",
                               f"a_tr{tt-2}_{NFC-1}" if tt >= 2 else None],
                        tag=f"a_norm{tt}")
                    for fc in range(NFC):
                        bk = fc % 2
                        P.op("tensor", lambda t, sl=sl, fc=fc, bk=bk: t.transpose(
                            pA[bk][:, :], xnst[:, sl, fc * 128:(fc + 1) * 128], ident[:, :]),
                            waits=[f"a_norm{tt}", "ld_id", trbank_last_ev[bk]],
                            tag=f"a_tr{tt}_{fc}")
                        P.op("vector", lambda v, tt=tt, fc=fc, bk=bk: v.tensor_copy(
                            xnT[:, fc, tt * 128:(tt + 1) * 128], pA[bk][:, :]),
                            waits=[f"a_tr{tt}_{fc}"], tag=f"a_tev{tt}_{fc}")
                        trbank_last_ev[bk] = f"a_tev{tt}_{fc}"
                if dbg_xnT is not None:
                    P.op("sync", lambda g: g.dma_start(dbg_xnT, xnT[:, :, :]),
                         waits=[f"a_tev{NTT-1}_{NFC-1}"], dma=True, tag="dbg_xnT")
                P.barrier("barA", [f"a_tev{NTT-1}_{NFC-1}", f"a_rstd{NTT-1}",
                                   f"a_tr{NTT-1}_{NFC-1}", f"a_x{NTT-1}"]
                          + (["dbg_xnT"] if dbg_xnT is not None else []))
                P.flush(nc, sems)

            with contextlib.ExitStack() as s2:
                sbb = lambda n, s, d: s2.enter_context(nc.sbuf_tensor(n, s, d))
                psb = lambda n, s, d: s2.enter_context(nc.psum_tensor(n, s, d))
                KT = [sbb(f"KT{i}", [128, T], BF16) for i in range(2)]
                QT = [sbb(f"QT{i}", [128, TOWN], BF16) for i in range(2)]
                Vo = [sbb(f"Vo{i}", [128, NTT, 130], BF16) for i in range(2)]
                vT = sbb("vT", [128, T], BF16)
                Pt = [sbb(f"Pt{i}", [128, NTT, 256], BF16) for i in range(2)]
                avtok = sbb("avtok", [128, NOWN, 128], BF16)
                raw = [sbb(f"raw{i}", [128, 512], BF16) for i in range(2)]
                m1 = sbb("m1", [128, 512], F32)
                m2 = sbb("m2", [128, 512], F32)
                dinv = sbb("dinv", [128, 1], F32)
                dens = sbb("dens", [128, NH, NOWN], F32) if dbg_den is not None else None

                bA = [psb(f"bA{i}", [128, 512], F32) for i in range(2)]
                bB = psb("bB", [128, 512], F32)
                pST = [psb(f"pST{i}", [128, 2, 256], F32) for i in range(2)]
                pAV = psb("pAV", [128, 132], F32)
                pTR = [psb(f"pTR{i}", [128, 128], BF16) for i in range(2)]

                for i in range(2):
                    P.op("vector", lambda v, i=i: v.memset(
                        ap3(Vo[i], 128, [[NTT * 130, 128], [130, NTT], [1, 1]]), 1.0),
                        tag=f"b_ones{i}")

                bA_last_rd = [None, None]
                bB_last_rd = [None]
                raw_last_rd = [None, None]
                vT_last_rd = [None] * 4
                pST_last_rd = [None, None]
                pAV_last_rd = [None]
                pTR_last_rd = [None, None]
                Pt_last_rd = [None, None]
                KTQT_last_rd = [None, None]
                Vo_last_rd = [None, None]
                avtok_last_rd = [None] * NOWN
                wbuf_last_rd = [None, None]
                ck = [0]
                spc = [0]

                pending = []

                def emit_pending():
                    if not pending:
                        return
                    kind, h2, c, bi, ri, ba, cc, ss, dst, dst_guard = pending.pop(0)
                    P.op("tensor", lambda t, ri=ri: t.matmul(
                        bB[:, :], perms[:, :], raw[ri][:, :], start=True, stop=True),
                        waits=[f"b_{kind}rw{h2}_{c}", "ld_pm", bB_last_rd[0]],
                        tag=f"b_{kind}sw{h2}_{c}")
                    raw_last_rd[ri] = f"b_{kind}sw{h2}_{c}"
                    ba3 = ap3(ba, 0, [[512, 128], [128, 4], [1, 128]])
                    bB3 = ap3(bB, 0, [[512, 128], [128, 4], [1, 128]])
                    m13 = ap3(m1, 0, [[512, 128], [128, 4], [1, 128]])
                    m23 = ap3(m2, 0, [[512, 128], [128, 4], [1, 128]])
                    P.op("vector", lambda v, m13=m13, ba3=ba3, cc=cc: v.tensor_tensor(
                        m13, ba3, cc, ALU.mult),
                        waits=["ld_cos"], tag=f"b_{kind}m1_{h2}_{c}")
                    bA_last_rd[bi] = f"b_{kind}m1_{h2}_{c}"
                    P.op("vector", lambda v, m23=m23, bB3=bB3, ss=ss: v.tensor_tensor(
                        m23, bB3, ss, ALU.mult),
                        waits=[f"b_{kind}sw{h2}_{c}", "ld_sin"], tag=f"b_{kind}m2_{h2}_{c}")
                    bB_last_rd[0] = f"b_{kind}m2_{h2}_{c}"
                    P.op("vector", lambda v, dst=dst: v.tensor_add(
                        dst, m1[:, :], m2[:, :]),
                        waits=[dst_guard, f"b_{kind}m1_{h2}_{c}", f"b_{kind}m2_{h2}_{c}"],
                        tag=f"b_{kind}dn{h2}_{c}")

                def proj_rope(h, hb, kind, wbuf, wtag, nchunks, rhs_of, cs_of, dst_of, dst_guard):
                    for c in range(nchunks):
                        bi = ck[0] % 2
                        ba = bA[bi]
                        for fc in range(NFC):
                            P.op("tensor",
                                 lambda t, ba=ba, rhs=rhs_of(c, fc), fc=fc, wbuf=wbuf: t.matmul(
                                     ba[:, :], wbuf[:, fc, :], rhs,
                                     start=(fc == 0), stop=(fc == NFC - 1)),
                                 waits=[wtag, bA_last_rd[bi]] if fc == 0 else [],
                                 tag=f"b_{kind}fl{h}_{c}" if fc == NFC - 1 else None)
                        ri = ck[0] % 2
                        P.op("vector", lambda v, ba=ba, ri=ri: v.tensor_copy(raw[ri][:, :], ba[:, :]),
                             waits=[f"b_{kind}fl{h}_{c}", raw_last_rd[ri]],
                             tag=f"b_{kind}rw{h}_{c}")
                        cc, ss = cs_of(c)
                        pending.append((kind, h, c, bi, ri, ba, cc, ss, dst_of(c), dst_guard))
                        if len(pending) > 1:
                            emit_pending()
                        ck[0] += 1

                for h in range(NH):
                    hb = h % 2
                    if h >= 2:
                        P.op("gpsimd", lambda g, h=h, hb=hb: g.dma_start(
                            wq[hb][:, :, :], wqi_h[h]),
                            waits=[wbuf_last_rd[hb]], dma=True, tag=f"b_wq{h}")
                        P.op("gpsimd", lambda g, h=h, hb=hb: g.dma_start(
                            wk[hb][:, :, :], wki_h[h]), dma=True, tag=f"b_wk{h}")
                        P.op("gpsimd", lambda g, h=h, hb=hb: g.dma_start(
                            wv[hb][:, :, :], wvi_h[h]), dma=True, tag=f"b_wv{h}")

                    proj_rope(
                        h, hb, "k", wk[hb], f"b_wk{h}", 4,
                        rhs_of=lambda c, fc: xnT[:, fc, c * 512:(c + 1) * 512],
                        cs_of=lambda c: (ap3(cosf, c * 512, [[T, 128], [128, 4], [1, 128]]),
                                         ap3(sinf, c * 512, [[T, 128], [128, 4], [1, 128]])),
                        dst_of=lambda c: KT[hb][:, c * 512:(c + 1) * 512],
                        dst_guard=KTQT_last_rd[hb])

                    proj_rope(
                        h, hb, "q", wq[hb], f"b_wq{h}", 2,
                        rhs_of=lambda c, fc: xnT[:, fc, c * 512:(c + 1) * 512],
                        cs_of=lambda c: (ap3(cosf, c * 512, [[T, 128], [128, 4], [1, 128]]),
                                         ap3(sinf, c * 512, [[T, 128], [128, 4], [1, 128]])),
                        dst_of=lambda c: QT[hb][:, c * 512:(c + 1) * 512],
                        dst_guard=KTQT_last_rd[hb])

                    for c in range(4):
                        bi = ck[0] % 2
                        ba = bA[bi]
                        for fc in range(NFC):
                            P.op("tensor", lambda t, ba=ba, fc=fc, c=c, hb=hb: t.matmul(
                                ba[:, :], wv[hb][:, fc, :], xnT[:, fc, c * 512:(c + 1) * 512],
                                start=(fc == 0), stop=(fc == NFC - 1)),
                                waits=[f"b_wv{h}", bA_last_rd[bi]] if fc == 0 else [],
                                tag=f"b_vfl{h}_{c}" if fc == NFC - 1 else None)
                        emit_pending()
                        P.op("vector", lambda v, ba=ba, c=c: v.tensor_copy(
                            vT[:, c * 512:(c + 1) * 512], ba[:, :]),
                            waits=[f"b_vfl{h}_{c}", vT_last_rd[c]], tag=f"b_vev{h}_{c}")
                        bA_last_rd[bi] = f"b_vev{h}_{c}"
                        ck[0] += 1
                    emit_pending()
                    for i in range(NTT):
                        bi = i % 2
                        P.op("tensor", lambda t, i=i, bi=bi: t.transpose(
                            pTR[bi][:, :], vT[:, i * 128:(i + 1) * 128], ident[:, :]),
                            waits=[f"b_vev{h}_{i // 4}", pTR_last_rd[bi]], tag=f"b_vtr{h}_{i}")
                        vT_last_rd[i // 4] = f"b_vtr{h}_{i}"
                        P.op("vector", lambda v, i=i, hb=hb, bi=bi: v.tensor_copy(
                            Vo[hb][:, i, 0:128], pTR[bi][:, :]),
                            waits=[f"b_vtr{h}_{i}", f"b_ones{hb}",
                                   Vo_last_rd[hb] if i == 0 else None],
                            tag=f"b_vte{h}_{i}")
                        pTR_last_rd[bi] = f"b_vte{h}_{i}"
                    wbuf_last_rd[hb] = f"b_vfl{h}_{3}"

                    def emit_avtr(h2, j):
                        bi = j % 2
                        P.op("tensor", lambda t, j=j, bi=bi: t.transpose(
                            pTR[bi][:, :], avtok[:, j, :], ident[:, :]),
                            waits=[f"b_ae{h2}_{j}", pTR_last_rd[bi]], tag=f"b_at{h2}_{j}")
                        avtok_last_rd[j] = f"b_at{h2}_{j}"
                        P.op("vector", lambda v, j=j, h2=h2, bi=bi: v.tensor_copy(
                            av_all[:, h2, j * 128:(j + 1) * 128], pTR[bi][:, :]),
                            waits=[f"b_at{h2}_{j}"], tag=f"b_ate{h2}_{j}")
                        pTR_last_rd[bi] = f"b_ate{h2}_{j}"

                    for J in range(NOWN // 2):
                        jq = Pt[J % 2]
                        ntl = 2 * J + 2
                        for pr in range(ntl):
                            i0, i1 = pr, NOWN + pr
                            sbk = spc[0] % 2
                            stb = pST[sbk]
                            fw = [pST_last_rd[sbk]]
                            if J == 0 and pr == 0:
                                fw += [f"b_kdn{h}_{3}", f"b_qdn{h}_{1}", f"b_vte{h}_{NTT-1}"]
                            P.op("tensor", lambda t, stb=stb, i0=i0, J=J, hb=hb: t.matmul(
                                stb[:, 0, :], KT[hb][:, i0 * 128:(i0 + 1) * 128],
                                QT[hb][:, J * 256:(J + 1) * 256], start=True, stop=True),
                                waits=fw)
                            P.op("tensor", lambda t, stb=stb, i1=i1, J=J, hb=hb: t.matmul(
                                stb[:, 1, :], KT[hb][:, i1 * 128:(i1 + 1) * 128],
                                QT[hb][:, J * 256:(J + 1) * 256], start=True, stop=True),
                                tag=f"b_sf{h}_{J}_{pr}")
                            KTQT_last_rd[hb] = f"b_sf{h}_{J}_{pr}"
                            stb2 = ap3(stb, 0, [[512, 128], [1, 512]])
                            P.op("scalar", lambda s, jq=jq, i0=i0, stb2=stb2: s.activation(
                                ap3(jq, i0 * 256,
                                    [[NTT * 256, 128], [NOWN * 256, 2], [1, 256]]),
                                stb2, AF.Exp, scale=ISC),
                                waits=[f"b_sf{h}_{J}_{pr}",
                                       Pt_last_rd[J % 2] if pr == 0 else None],
                                tag=f"b_ex{h}_{J}_{pr}")
                            pST_last_rd[sbk] = f"b_ex{h}_{J}_{pr}"
                            spc[0] += 1
                        P.op("vector", lambda v, jq=jq, J=J: v.tensor_mul(
                            jq[:, 2 * J, :], jq[:, 2 * J, :], maska[:, :]),
                            waits=[f"b_ex{h}_{J}_{ntl-1}", "ld_ma"])
                        P.op("vector", lambda v, jq=jq, J=J: v.tensor_mul(
                            jq[:, 2 * J + 1, :], jq[:, 2 * J + 1, :], maskb[:, :]),
                            waits=["ld_mb"])
                        P.op("vector", lambda v, jq=jq, J=J: v.tensor_mul(
                            jq[:, NOWN + 2 * J, :], jq[:, NOWN + 2 * J, :], maskc[:, :]),
                            waits=["ld_mc"])
                        P.op("vector", lambda v, jq=jq, J=J: v.tensor_mul(
                            jq[:, NOWN + 2 * J + 1, :], jq[:, NOWN + 2 * J + 1, :], maskd[:, :]),
                            waits=["ld_md"], tag=f"b_mk{h}_{J}")
                        for sh in range(2):
                            j = 2 * J + sh
                            tl = list(range(2 * J + sh + 1)) +                                  [NOWN + t for t in range(2 * J + sh + 1)]
                            nt = len(tl)
                            for idx, i in enumerate(tl):
                                pr = i if i < NOWN else i - NOWN
                                avw = []
                                if idx == 0:
                                    avw.append(pAV_last_rd[0])
                                if i in (2 * J, 2 * J + 1, NOWN + 2 * J, NOWN + 2 * J + 1):
                                    avw.append(f"b_mk{h}_{J}")
                                else:
                                    avw.append(f"b_ex{h}_{J}_{pr}")
                                P.op("tensor", lambda t, i=i, jq=jq, sh=sh, hb=hb,
                                     st=(idx == 0), sp=(idx == nt - 1): t.matmul(
                                    pAV[:, 0:129], jq[:, i, sh * 128:sh * 128 + 128],
                                    Vo[hb][:, i, 0:129], start=st, stop=sp),
                                    waits=avw, tag=f"b_av{h}_{j}" if idx == nt - 1 else None)
                            Vo_last_rd[hb] = f"b_av{h}_{j}"
                            if dens is not None:
                                P.op("vector", lambda v, h=h, j=j: v.tensor_copy(
                                    dens[:, h, j:j + 1], pAV[:, 128:129]),
                                    waits=[f"b_av{h}_{j}"])
                            P.op("vector", lambda v: v.reciprocal(dinv[:, :], pAV[:, 128:129]),
                                 waits=[f"b_av{h}_{j}"], tag=f"b_rc{h}_{j}")
                            P.op("vector", lambda v, j=j: v.tensor_scalar(
                                avtok[:, j, :], pAV[:, 0:128], dinv[:, :], None, ALU.mult),
                                waits=[avtok_last_rd[j], f"b_rc{h}_{j}"], tag=f"b_ae{h}_{j}")
                            pAV_last_rd[0] = f"b_ae{h}_{j}"
                            if j >= 1:
                                emit_avtr(h, j - 1)
                        Pt_last_rd[J % 2] = f"b_av{h}_{2 * J + 1}"
                    emit_avtr(h, NOWN - 1)

                if dbg_den is not None:
                    P.op("sync", lambda g: g.dma_start(dbg_den, dens[:, :, :]),
                         waits=[f"b_ae{NH-1}_{NOWN-1}"], dma=True, tag="dbg_den")
                if dbg_pt is not None:
                    P.op("sync", lambda g: g.dma_start(dbg_pt, Pt[1][:, :, :]),
                         waits=[f"b_ae{NH-1}_{NOWN-1}"], dma=True, tag="dbg_pt")
                if dbg_kt is not None:
                    P.op("sync", lambda g: g.dma_start(dbg_kt, KT[(NH - 1) % 2][:, :]),
                         waits=[f"b_kdn{NH-1}_{3}"], dma=True, tag="dbg_kt")
                if dbg_qt is not None:
                    P.op("sync", lambda g: g.dma_start(dbg_qt, QT[(NH - 1) % 2][:, :]),
                         waits=[f"b_qdn{NH-1}_{1}"], dma=True, tag="dbg_qt")
                if dbg_av is not None:
                    P.op("sync", lambda g: g.dma_start(dbg_av, av_all[:, :, :]),
                         waits=[f"b_ate{NH-1}_{NOWN-1}"], dma=True, tag="dbg_av")
                P.barrier("barB", [f"b_ate{NH-1}_{NOWN-1}", f"b_ex{NH-1}_{NOWN//2-1}_{NOWN-1}",
                                   f"b_at{NH-1}_{NOWN-1}", f"b_wv{NH-1}"]
                          + [t for t in ("dbg_kt", "dbg_qt", "dbg_av", "dbg_den", "dbg_pt") if t in P.tags])
                P.flush(nc, sems)

        with contextlib.ExitStack() as s3:
            sbc = lambda n, s, d: s3.enter_context(nc.sbuf_tensor(n, s, d))
            psc = lambda n, s, d: s3.enter_context(nc.psum_tensor(n, s, d))
            wob = sbc("wob", [128, NFC, 4, 512], BF16)
            xstage = sbc("xstage", [128, 2, D], F32)
            x1t = sbc("x1t", [128, 2, D], F32)
            xn2st = sbc("xn2st", [128, 2, D], BF16)
            sqscr2 = sbc("sqscr2", [128, D], F32)
            stats2 = sbc("stats2", [128, 8], F32)
            pC = [psc(f"pC{i}", [128, 512], F32) for i in range(2)]
            pTc = [psc(f"pTc{i}", [128, 128], BF16) for i in range(2)]

            for dc in range(4):
                P.op("gpsimd", lambda g, dc=dc: g.dma_start(
                    wob[:, :, dc, :], woi_h[dc]), dma=True, tag=f"c_wo{dc}")

            pC_last_rd = [None, None]
            pTc_last_rd = [None, None]
            xn2st_last_rd = [None, None]
            def emit_tr(j):
                sl = j % 2
                for fc in range(NFC):
                    bi = fc % 2
                    P.op("tensor", lambda t, sl=sl, fc=fc, bi=bi, j=j: t.transpose(
                        pTc[bi][:, :], xn2st[:, sl, fc * 128:(fc + 1) * 128], ident[:, :]),
                        waits=[f"d_norm{j}", pTc_last_rd[bi]], tag=f"d_tr{j}_{fc}")
                    P.op("vector", lambda v, j=j, fc=fc, bi=bi: v.tensor_copy(
                        xn2T[:, fc, j * 128:(j + 1) * 128], pTc[bi][:, :]),
                        waits=[f"d_tr{j}_{fc}"], tag=f"d_te{j}_{fc}")
                    pTc_last_rd[bi] = f"d_te{j}_{fc}"
                xn2st_last_rd[sl] = f"d_tr{j}_{NFC-1}"

            for j in range(NOWN):
                sl = j % 2
                P.op("sync", lambda sy, sl=sl, j=j: sy.dma_start(
                    xstage[:, sl, :], xp_h[j * 128:(j + 1) * 128, :]),
                    waits=[f"c_ev{j-2}_{3}"] if j >= 2 else [], dma=True, tag=f"c_x{j}")
                for dc in range(4):
                    bi = dc % 2
                    fw = [f"c_wo{dc}", pC_last_rd[bi], f"c_x{j}"]
                    for h in range(NH):
                        P.op("tensor", lambda t, h=h, j=j, dc=dc, bi=bi: t.matmul(
                            pC[bi][:, :], av_all[:, h, j * 128:(j + 1) * 128],
                            wob[:, h, dc, :], start=(h == 0), stop=(h == NH - 1)),
                            waits=fw if h == 0 else [],
                            tag=f"c_fl{j}_{dc}" if h == NH - 1 else None)
                    P.op("vector", lambda v, j=j, dc=dc, bi=bi, sl=sl: v.tensor_add(
                        x1t[:, sl, dc * 512:(dc + 1) * 512], pC[bi][:, :],
                        xstage[:, sl, dc * 512:(dc + 1) * 512]),
                        waits=[f"c_fl{j}_{dc}",
                               f"d_norm{j-2}" if (j >= 2 and dc == 0) else None,
                               f"c_st{j-2}" if (j >= 2 and dc == 0) else None],
                        tag=f"c_ev{j}_{dc}")
                    pC_last_rd[bi] = f"c_ev{j}_{dc}"
                P.op("sync", lambda sy, j=j, sl=sl: sy.dma_start(
                    x1_h[j * 128:(j + 1) * 128, :], x1t[:, sl, :]),
                    waits=[f"c_ev{j}_{3}"], dma=True, tag=f"c_st{j}")
                P.op("scalar", lambda s, sl=sl: s.activation(
                    sqscr2[:, :], x1t[:, sl, :], AF.Square, accum_out=stats2[:, 1:2]),
                    waits=[f"c_ev{j}_{3}", f"d_ex2_{j-1}" if j >= 1 else None],
                    tag=f"d_sq{j}")
                P.op("vector", lambda v, sl=sl: v.reduce_sum(
                    stats2[:, 0:1], x1t[:, sl, :], axis=AX.X),
                    waits=[f"c_ev{j}_{3}"], tag=f"d_r{j}")
                P.op("vector", lambda v: v.tensor_scalar_mul(stats2[:, 2:3], stats2[:, 0:1], 1.0 / D),
                     waits=[f"d_r{j}"], tag=f"d_c2_{j}")
                P.op("vector", lambda v: v.tensor_scalar_mul(stats2[:, 3:4], stats2[:, 1:2], 1.0 / D),
                     waits=[f"d_sq{j}"], tag=f"d_ex2_{j}")
                P.op("vector", lambda v: v.tensor_mul(stats2[:, 4:5], stats2[:, 2:3], stats2[:, 2:3]),
                     waits=[f"d_c2_{j}"], tag=f"d_c4_{j}")
                P.op("vector", lambda v: v.tensor_sub(stats2[:, 5:6], stats2[:, 3:4], stats2[:, 4:5]),
                     waits=[f"d_ex2_{j}", f"d_c4_{j}"], tag=f"d_c5_{j}")
                P.op("vector", lambda v: v.tensor_scalar_add(stats2[:, 6:7], stats2[:, 5:6], 1e-5),
                     waits=[f"d_c5_{j}"], tag=f"d_c6_{j}")
                P.op("vector", lambda v: v.reciprocal(stats2[:, 7:8], stats2[:, 6:7]),
                     waits=[f"d_c6_{j}"], tag=f"d_rec{j}")
                P.op("scalar", lambda s: s.activation(stats2[:, 6:7], stats2[:, 7:8], AF.Sqrt),
                     waits=[f"d_rec{j}"], tag=f"d_rstd{j}")
                P.op("vector", lambda v, sl=sl: v.tensor_scalar(
                    xn2st[:, sl, :], x1t[:, sl, :], stats2[:, 2:3], stats2[:, 6:7],
                    ALU.subtract, ALU.mult),
                    waits=[f"d_rstd{j}", f"d_c2_{j}", xn2st_last_rd[sl]], tag=f"d_norm{j}")
                if j >= 1:
                    emit_tr(j - 1)
            emit_tr(NOWN - 1)
            if dbg_xn2 is not None:
                P.op("sync", lambda g: g.dma_start(dbg_xn2, xn2T[:, :, :]),
                     waits=[f"d_te{NOWN-1}_{NFC-1}"], dma=True, tag="dbg_xn2")
            P.barrier("barC", [f"d_te{NOWN-1}_{NFC-1}", f"d_rstd{NOWN-1}",
                               f"d_tr{NOWN-1}_{NFC-1}", f"c_st{NOWN-1}", f"c_wo{3}"]
                      + (["dbg_xn2"] if dbg_xn2 is not None else []))
            P.flush(nc, sems)

        with contextlib.ExitStack() as s4:
            sbe = lambda n, s, d: s4.enter_context(nc.sbuf_tensor(n, s, d))
            pse = lambda n, s, d: s4.enter_context(nc.psum_tensor(n, s, d))
            act = sbe("acts", [128, NHC, TOWN], FP8)
            wgb = [[sbe(f"wg{g}_{q}", [128, NFC, HCG * 128], FP8) for q in range(2)] for g in range(2)]
            wlb = [sbe(f"wl{q}", [128, NHC, 256], FP8) for q in range(2)]
            gate = sbe("gate", [128, 2, TOWN], BF16)
            outs = sbe("outs", [128, 2, 256], F32)
            x1p = sbe("x1p", [128, 2, 256], F32)
            pg = [pse(f"pg{i}", [128, 512], F32) for i in range(4)]
            pl = [pse(f"pl{i}", [128, 256], F32) for i in range(4)]

            def gfill(grp, hc_i, gg, n):
                return f"e_gf{grp}_{hc_i}_{gg}_{n}"

            pg_last_rd = [None] * 4
            gate_last_rd = [None, None]
            for grp in range(NGRP):
                for gg in range(2):
                    P.op("gpsimd", lambda g, grp=grp, gg=gg: g.dma_start(
                        wgb[gg][grp % 2][:, :, :], wgi_h[grp, gg]),
                        waits=[gfill(grp - 2, HCG - 1, gg, 1) if grp >= 2 else None],
                        dma=True, tag=f"e_wg{grp}_{gg}")
                for hc_i in range(HCG):
                    hc = grp * HCG + hc_i
                    for gg in range(2):
                        for n in range(2):
                            bi = gg * 2 + n
                            fw = [f"e_wg{grp}_{gg}", pg_last_rd[bi]]
                            if grp == 0 and hc_i == 0 and gg == 0 and n == 0:
                                fw.append(f"d_te{NOWN-1}_{NFC-1}")
                            for a in range(NFC // 2):
                                lw = ap3(wgb[gg][grp % 2], (2 * a) * (HCG * 128) + hc_i * 128,
                                         [[NFC * HCG * 128, 128], [HCG * 128, 2], [1, 128]])
                                rw = ap3(xn2T, (2 * a) * TOWN + n * 512,
                                         [[NFC * TOWN, 128], [TOWN, 2], [1, 512]])
                                P.op("tensor",
                                     lambda t, bi=bi, lw=lw, rw=rw, a=a: t.matmul(
                                         pg[bi][:, :], lw, rw,
                                         start=(a == 0), stop=(a == NFC // 2 - 1),
                                         perf_mode=mybir.MatmulPerfMode.DoubleRow),
                                     waits=fw if a == 0 else [],
                                     tag=gfill(grp, hc_i, gg, n) if a == NFC // 2 - 1 else None)
                    for n in range(2):
                        P.op("scalar", lambda s, n=n, hc=hc: s.activation(
                            gate[:, hc % 2, n * 512:(n + 1) * 512], pg[n][:, :],
                            AF.Gelu_apprx_tanh),
                            waits=[gfill(grp, hc_i, 0, n),
                                   gate_last_rd[hc % 2] if n == 0 else None],
                            tag=f"e_gl{grp}_{hc_i}_{n}")
                        pg_last_rd[n] = f"e_gl{grp}_{hc_i}_{n}"
                        P.op("vector", lambda v, n=n, hc=hc: v.tensor_tensor(
                            act[:, hc, n * 512:(n + 1) * 512],
                            gate[:, hc % 2, n * 512:(n + 1) * 512],
                            pg[2 + n][:, :], ALU.mult),
                            waits=[gfill(grp, hc_i, 1, n), f"e_gl{grp}_{hc_i}_{n}"],
                            tag=f"e_ac{grp}_{hc_i}_{n}")
                        pg_last_rd[2 + n] = f"e_ac{grp}_{hc_i}_{n}"
                    gate_last_rd[hc % 2] = f"e_ac{grp}_{hc_i}_{1}"

            pl_last_rd = [None] * 4
            outs_last_rd = [None, None]
            lf = 0
            for dc in range(8):
                P.op("gpsimd", lambda g, dc=dc: g.dma_start(
                    wlb[dc % 2][:, :, :], wli_h[dc]),
                    waits=[f"e_lf{dc-2}_{NOWN-1}" if dc >= 2 else None],
                    dma=True, tag=f"e_wl{dc}")
                for tt in range(NOWN):
                    bi = lf % 4
                    fw = [f"e_wl{dc}", pl_last_rd[bi]]
                    if dc == 0 and tt == 0:
                        fw.append(f"e_ac{NGRP-1}_{HCG-1}_{1}")
                    for hc in range(NHC):
                        P.op("tensor", lambda t, hc=hc, tt=tt, dc=dc, bi=bi: t.matmul(
                            pl[bi][:, :], act[:, hc, tt * 128:(tt + 1) * 128],
                            wlb[dc % 2][:, hc, :], start=(hc == 0), stop=(hc == NHC - 1)),
                            waits=fw if hc == 0 else [],
                            tag=f"e_lf{dc}_{tt}" if hc == NHC - 1 else None)
                    P.op("sync", lambda sy, tt=tt, dc=dc, sl=lf % 2: sy.dma_start(
                        x1p[:, sl, :], x1_h[tt * 128:(tt + 1) * 128, dc * 256:(dc + 1) * 256]),
                        waits=[f"c_st{tt}", f"e_ev{lf-2}" if lf >= 2 else None],
                        dma=True, tag=f"e_x1{lf}")
                    P.op("vector", lambda v, bi=bi, sl=lf % 2: v.tensor_add(
                        outs[:, sl, :], pl[bi][:, :], x1p[:, sl, :]),
                        waits=[f"e_lf{dc}_{tt}", f"e_x1{lf}", outs_last_rd[lf % 2]],
                        tag=f"e_ev{lf}")
                    pl_last_rd[bi] = f"e_ev{lf}"
                    P.op("sync", lambda sy, tt=tt, dc=dc, sl=lf % 2: sy.dma_start(
                        out_h[tt * 128:(tt + 1) * 128, dc * 256:(dc + 1) * 256],
                        outs[:, sl, :]),
                        waits=[f"e_ev{lf}"], dma=True, tag=f"e_st{lf}")
                    outs_last_rd[lf % 2] = f"e_st{lf}"
                    lf += 1
            P.op("gpsimd", "W", waits=[f"e_st{63}"])
            P.op("vector", "W", waits=[f"e_st{63}"])
            P.op("tensor", "W", waits=[f"e_st{63}"])
            P.op("scalar", "W", waits=[f"e_st{63}"])
            P.flush(nc, sems)

    return nc, sorted(dbg_aps.keys())


def host_constants():
    inv_freq = (1.0 / (10000.0 ** (np.arange(0, HD, 2, dtype=np.float32) / HD))).astype(np.float32)
    t = np.arange(T, dtype=np.float32)
    freqs = np.outer(t, inv_freq).astype(np.float32)
    cos64 = np.cos(freqs).astype(np.float32)
    sin64 = np.sin(freqs).astype(np.float32)
    cosf = np.ascontiguousarray(np.concatenate([cos64, cos64], axis=1).T)
    sinf = np.ascontiguousarray(np.concatenate([sin64, sin64], axis=1).T)
    perms = np.zeros((128, 128), dtype=np.float32)
    for m in range(64):
        perms[m + 64, m] = -1.0
    for m in range(64, 128):
        perms[m - 64, m] = 1.0
    ident = np.eye(128, dtype=np.float32)
    tri = (np.arange(128)[:, None] <= np.arange(128)[None, :]).astype(np.float32)
    return cosf, sinf, perms, ident, tri


def weight_images(wqkv, wo, wg, wl):
    w3 = wqkv.reshape(NFC, 128, 3 * NH, 128)
    wqi = np.ascontiguousarray(w3[:, :, 0:NH].transpose(2, 1, 0, 3))
    wki = np.ascontiguousarray(w3[:, :, NH:2 * NH].transpose(2, 1, 0, 3))
    wvi = np.ascontiguousarray(w3[:, :, 2 * NH:3 * NH].transpose(2, 1, 0, 3))
    woi = np.ascontiguousarray(wo.reshape(NFC, 128, 4, 512).transpose(2, 1, 0, 3))
    wgi = np.ascontiguousarray(
        wg.reshape(2, NFC, 128, NGRP, HCG * 128).transpose(3, 0, 2, 1, 4))
    wli = np.ascontiguousarray(wl.reshape(NHC, 128, 8, 256).transpose(2, 1, 0, 3))
    return {"wqi": wqi, "wki": wki, "wvi": wvi, "woi": woi, "wgi": wgi, "wli": wli}


def core_inputs(p, xb, wimgs):
    cosf, sinf, perms, ident, tri = host_constants()
    perm = np.concatenate([np.arange(16)[p::2], np.arange(16)[1 - p::2]])
    xp = np.ascontiguousarray(xb.reshape(16, 128, D)[perm].reshape(T, D))
    cosp = np.ascontiguousarray(cosf.reshape(128, 16, 128)[:, perm].reshape(128, T))
    sinp = np.ascontiguousarray(sinf.reshape(128, 16, 128)[:, perm].reshape(128, T))
    one = np.ones((128, 128), dtype=np.float32)
    zero = np.zeros((128, 128), dtype=np.float32)
    maska = np.concatenate([tri, one], axis=1)
    maskb = np.concatenate([zero, tri], axis=1)
    if p == 0:
        maskc = np.concatenate([zero, one], axis=1)
        maskd = np.concatenate([zero, zero], axis=1)
    else:
        maskc = np.concatenate([one, one], axis=1)
        maskd = np.concatenate([zero, one], axis=1)
    d = {
        "xp": xp,
        "cosf": cosp, "sinf": sinp, "perms": perms, "ident": ident,
        "maska": maska, "maskb": maskb, "maskc": maskc, "maskd": maskd,
    }
    d.update(wimgs)
    return d


_NC_CACHE = {}


def _get_nc():
    if "nc" not in _NC_CACHE:
        _NC_CACHE["nc"] = build_core_program(0, dbg=())[0]
    return _NC_CACHE["nc"]


def kernel(L0_attn_x, L0_attn_w_qkv, L0_attn_w_o, L0_ffn_w_gating, L0_ffn_w_linear):
    from concourse.bass_utils import run_bass_kernel_spmd
    x = np.ascontiguousarray(np.asarray(L0_attn_x, dtype=np.float32))
    wqkv = np.ascontiguousarray(np.asarray(L0_attn_w_qkv, dtype=np.float32))
    wo = np.ascontiguousarray(np.asarray(L0_attn_w_o, dtype=np.float32))
    wg = np.ascontiguousarray(np.asarray(L0_ffn_w_gating, dtype=np.float32))
    wl = np.ascontiguousarray(np.asarray(L0_ffn_w_linear, dtype=np.float32))

    nc = _get_nc()
    wimgs = weight_images(wqkv, wo, wg, wl)
    in_maps = []
    for c in range(8):
        b, p = c // 2, c % 2
        in_maps.append(core_inputs(p, np.ascontiguousarray(x[b]), wimgs))
    res = run_bass_kernel_spmd(nc, in_maps, core_ids=list(range(8)))

    out = np.empty((4, 2048, 2048), dtype=np.float32)
    for c in range(8):
        b, p = c // 2, c % 2
        o = np.asarray(res.results[c]["out"]).reshape(NOWN, 128, D)
        out[b].reshape(16, 128, D)[p::2] = o
    return out
